# version 09
# baseline (speedup 1.0000x reference)
import numpy as np
import concourse.bass as bass
import concourse.mybir as mybir
from concourse.bass_utils import run_bass_kernel_spmd

F32 = mybir.dt.float32
F32R = mybir.dt.float32r
F16 = mybir.dt.float16
AF = mybir.ActivationFunctionType
ALU = mybir.AluOpType

B, L, DM, N, RK = 4, 2048, 4096, 16, 128
NL, LT = 4, 512
CH = 32
OC = 32
NS = 1436
EPS = 1e-5

ENGINES = ("sync", "gpsimd", "vector", "scalar", "tensor")


class Sched:
    def __init__(self):
        self.ops = {e: [] for e in ENGINES}
        self.cnt = {}

    def _inc(self, sem, amt):
        self.cnt[sem] = self.cnt.get(sem, 0) + amt
        return self.cnt[sem]

    def add(self, eng, fn, waits=(), inc=None, dmaidx=None):
        tok = None
        if inc is not None:
            tok = (inc[0], self._inc(inc[0], inc[1]))
        self.ops[eng].append((fn, [w for w in waits if w is not None], inc,
                              dmaidx))
        return tok

    def barrier(self):
        dmaw = [(s, self.cnt.get(s, 0)) for s in sorted(self.dma_sems)]
        toks = {}
        for e in ENGINES:
            toks[e] = self.add(e, lambda eng: None, waits=dmaw,
                               inc=("B_" + e, 1))
        for e in ENGINES:
            self.add(e, lambda eng: None,
                     waits=[toks[o] for o in ENGINES if o != e])


DEBUG = False


def build_nc(dma_amts=None, probe=None):
    nc = bass.Bass("TRN2", target_bir_lowering=False)
    dbg_kind = "ExternalOutput" if DEBUG else "Internal"
    S = Sched()
    S.dma_amts = dma_amts or {}
    S.dma_n = 0
    S.dma_sems = set()

    def din(name, shape, dt):
        return nc.dram_tensor(name, shape, dt, kind="ExternalInput").ap()

    xTr_h = din("xTr", [DM, L], F32R)
    xTf_h = din("xTf", [DM, L], F32)
    xpwT_h = din("xpwT", [DM, 160], F32R)
    xpwTF_h = din("xpwTF", [DM, 160], F32)
    dtwT_h = din("dtwT", [RK, DM], F32R)
    dtb_h = din("dtb", [128, CH], F32)
    alogT_h = din("alogT", [128, CH * N], F32)
    dD_h = din("dD", [128, CH], F32)
    wt4_h = din("wt4", [OC, 128, CH, 128], F32)
    b_h = din("brow", [128, CH], F32)
    acat3_h = din("acat3", [128, CH, 128], F32)
    bcat_h = din("bcat", [128, CH, 128], F32)
    rw3_h = din("rw3", [128, CH * 8], F32)
    rb_h = din("rbrow", [1, 8], F32)
    maskbc_h = din("maskbc", [128, L], F16)
    idF_h = din("idF", [128, 128], F32)
    idH_h = din("idH", [128, 128], F16)
    on1F_h = din("on1F", [1, 128], F32)
    on1H_h = din("on1H", [1, 128], F16)
    onesrow_h = din("onesrow", [1, LT], F32)
    oc1R_h = din("oc1R", [128, 1], F32)
    ocmR_h = din("ocmR", [128, 1], F32R)
    ocmH_h = din("ocmH", [128, 1], F16)
    ocmF_h = din("ocmF", [128, 1], F32)
    selbc_h = din("selbc", [128, 8], F32)

    outT_h = nc.dram_tensor("outT", [DM, L], F32, kind="ExternalOutput").ap()
    x1sp_h = nc.dram_tensor("x1sp", [DM, L], F16, kind=dbg_kind).ap()
    rows_h = nc.dram_tensor("rowsd", [8, L], F32, kind=dbg_kind).ap()
    bct_h2 = nc.dram_tensor("bctd", [32, L], F16, kind=dbg_kind).ap()

    import contextlib
    ctx = contextlib.ExitStack()

    def sb(name, shape, dt):
        return ctx.enter_context(nc.sbuf_tensor(name, shape, dt))

    dtw_ring = sb("dtw_ring", [RK, 2, 128], F32R)
    negA = sb("negA", [128, CH * N], F32)
    Dt_s = sb("Dt_s", [128, CH], F32)
    AcatH = sb("AcatH", [128, CH, 128], F16)
    BcatH = sb("BcatH", [128, CH, 128], F16)
    rW_s = sb("rW_s", [128, CH * 8], F32)
    dtb_s = sb("dtb_s", [128, CH], F32)
    b_s = sb("b_s", [128, CH], F32)
    rb_s = sb("rb_s", [1, 8], F32)
    maskbc_s = sb("maskbc_s", [128, L], F16)
    idF_s = sb("idF_s", [128, 128], F32)
    idH_s = sb("idH_s", [128, 128], F16)
    on1F_s = sb("on1F_s", [1, 128], F32)
    on1H_s = sb("on1H_s", [1, 128], F16)
    onesrow_s = sb("onesrow_s", [1, LT], F32)
    oc1R_s = sb("oc1R_s", [128, 1], F32)
    ocmR_s = sb("ocmR_s", [128, 1], F32R)
    ocmH_s = sb("ocmH_s", [128, 1], F16)
    ocmF_s = sb("ocmF_s", [128, 1], F32)
    selbc_s = sb("selbc_s", [128, 8], F32)

    xpw_ring = sb("xpw_ring", [128, 2, 160], F32R)
    xpwF_ring = sb("xpwF_ring", [128, 2, 160], F32)
    xtr_ring = sb("xtr_ring", [128, 2, LT], F32R)
    xtf_ring = sb("xtf_ring", [128, 2, LT], F32)

    rowm = sb("rowm", [1, LT], F32)
    rowt = sb("rowt", [1, LT], F32)
    rowt2 = sb("rowt2", [1, LT], F32)

    xpraw_dt = sb("xpraw_dt", [128, NL, LT], F16)
    xpraw_bc = sb("xpraw_bc", [32, NL, LT], F16)
    negw1a = sb("negw1a", [128, 1], F32)
    negw1b = sb("negw1b", [32, 1], F32)

    mbc1 = sb("mbc1", [128, LT], F32)
    rstdbc1 = sb("rstdbc1", [128, LT], F32)
    mrbc1 = sb("mrbc1", [128, LT], F32)
    dt_inT_t = sb("dt_inT_t", [RK, LT], F32R)
    dt_inT = dt_inT_t[:]
    BCT = sb("BCT", [32, LT], F16)
    B_bc = sb("B_bc", [128, N, LT], F16)
    C_bc = sb("C_bc", [128, N, LT], F16)

    dtT_ring = sb("dtT_ring", [128, 2, LT], F32)
    xnT_ring = sb("xnT_ring", [128, 2, LT], F16)
    xn_t3 = sb("xn_t3", [128, LT], F16)
    u_ring = sb("u_ring", [128, 2, LT], F16)
    dA_ring = sb("dA_ring", [128, 2, LT], F32)
    dBx_ring = sb("dBx_ring", [128, 2, LT], F16)
    h_ring = sb("h_ring", [128, 2, LT], F16)
    ytmp_ring = sb("ytmp_ring", [128, 2, LT], F16)
    hst = sb("hst", [128, CH * N], F32)
    x1c_ring = sb("x1c_ring", [128, 2, LT], F16)
    sq2_ring = sb("sq2_ring", [128, 2, LT], F16)

    m2bc = sb("m2bc", [128, LT], F32)
    rstd2bc = sb("rstd2bc", [128, LT], F32)
    znT = sb("znT", [128, CH, LT], F16)
    x1rr_ring = sb("x1rr_ring", [128, 2, LT], F16)
    q_acc = sb("q_acc", [128, CH], F32)
    qpart = sb("qpart", [128, 1], F32)

    Wf = sb("Wf", [128, CH // 2, 128], F32)
    Wh = sb("Wh", [128, CH, 128], F16)
    T_T = sb("T_T", [128, L], F16)
    stg_ring = sb("stg_ring", [128, 2, LT], F32)
    dtin_t1 = stg_ring[0:RK, 0, :]
    bct_t1 = stg_ring[0:32, 1, :]
    xpdt_acc = stg_ring[:, 0, :]
    xpbc_acc = stg_ring[0:32, 1, :]
    w1a_acc = stg_ring[:, 0, 0:1]
    w1b_acc = stg_ring[0:32, 1, 0:1]
    rbf_ring = stg_ring
    wb_ring = sb("wb_ring", [128, 2, LT], F32)
    routing = sb("routing", [1, 8], F32)
    logit_s = sb("logit_s", [1, 8], F32)
    sm_t = sb("sm_t", [1, 8], F32)
    sm_m = sb("sm_m", [1, 1], F32)
    sm_s = sb("sm_s", [1, 1], F32)
    routing_bc = sb("routing_bc", [128, 8], F32)
    rexp_t = sb("rexp_t", [128, 8], F32)
    rexp = sb("rexp", [128, 1], F32)

    PA = ctx.enter_context(nc.psum_tensor("PA", [128, LT], F32))
    PB = ctx.enter_context(nc.psum_tensor("PB", [128, LT], F32))
    PC = ctx.enter_context(nc.psum_tensor("PC", [128, LT], F32))
    PD = ctx.enter_context(nc.psum_tensor("PD", [128, LT], F32))
    PE_ = ctx.enter_context(nc.psum_tensor("PE_", [128, LT], F32))
    PF = ctx.enter_context(nc.psum_tensor("PF", [128, LT], F32))

    sems = {}

    def _sem(s):
        if s not in sems:
            sems[s] = ctx.enter_context(nc.semaphore(s))
        return sems[s]

    def dma(q, out, in_, waits=(), sem="LD"):
        idx = S.dma_n
        S.dma_n += 1
        amt = S.dma_amts.get(idx, 16)
        S.dma_sems.add(sem)
        return S.add(q, lambda e, o=out, i=in_: e.dma_start(out=o, in_=i),
                     waits=waits, inc=(sem, amt), dmaidx=idx)

    def ve(fn, waits=()):
        return S.add("vector", fn, waits=waits, inc=("VE", 1))

    def sc(fn, waits=()):
        return S.add("scalar", fn, waits=waits, inc=("SC", 1))

    def te(fn, waits=()):
        return S.add("tensor", fn, waits=waits, inc=("TE", 1))

    loads = [
        dma("sync", negA[:], alogT_h),
        dma("sync", Dt_s[:], dD_h),
        dma("sync", dtb_s[:], dtb_h),
        dma("sync", b_s[:], b_h),
        dma("sync", rb_s[:], rb_h),
        dma("sync", maskbc_s[:], maskbc_h),
        dma("sync", idF_s[:], idF_h),
        dma("sync", idH_s[:], idH_h),
        dma("sync", on1F_s[:], on1F_h),
        dma("sync", on1H_s[:], on1H_h),
        dma("sync", onesrow_s[:], onesrow_h),
        dma("sync", oc1R_s[:], oc1R_h),
        dma("sync", ocmR_s[:], ocmR_h),
        dma("sync", ocmH_s[:], ocmH_h),
        dma("sync", ocmF_s[:], ocmF_h),
        dma("sync", selbc_s[:], selbc_h),
        dma("sync", rW_s[:], rw3_h),
    ]
    lw = dma("sync", Wf[:], acat3_h[:, 0:CH // 2, :], sem="LA")
    loads.append(lw)
    S.barrier()
    wait_pro = []
    ta1 = sc(lambda e: e.copy(out=AcatH[:, 0:CH // 2], in_=Wf[:]), waits=[lw])
    lw2 = dma("sync", Wf[:], acat3_h[:, CH // 2:CH, :], waits=[ta1], sem="LA")
    sc(lambda e: e.copy(out=AcatH[:, CH // 2:CH], in_=Wf[:]), waits=[lw2])
    t_expA = sc(lambda e: e.activation(out=negA[:], in_=negA[:], func=AF.Exp),
                waits=wait_pro)
    ve(lambda e: e.tensor_scalar(out=negA[:], in0=negA[:], scalar1=-1.0,
                                 scalar2=None, op0=ALU.mult), waits=[t_expA])
    ve(lambda e: e.memset(hst[:], 0.0))
    ve(lambda e: e.memset(q_acc[:], 0.0))
    tok_xpwl, tok_w1m, tok_w1a = {}, {}, {}
    ve(lambda e: e.memset(w1a_acc, 0.0))
    ve(lambda e: e.memset(w1b_acc, 0.0))
    for c in range(CH):
        pr = tok_w1m.get(c - 2)
        tok_xpwl[c] = dma("sync", xpwF_ring[:, c % 2, :],
                          xpwTF_h[128 * c:128 * (c + 1), :],
                          waits=[pr] if pr else [], sem="LPF%d" % (c % 2))
        ga = tok_w1a.get(c - 1)
        tok_w1m[c] = te(lambda e, c=c: (
            e.matmul(PA[:, 0:1], xpwF_ring[:, c % 2, 0:128], oc1R_s[:],
                     start=True, stop=True),
            e.matmul(PB[0:32, 0:1], xpwF_ring[:, c % 2, 128:160], oc1R_s[:],
                     start=True, stop=True))[-1],
            waits=[tok_xpwl[c]] + ([ga] if ga else []))
        ve(lambda e: e.tensor_tensor(out=w1a_acc, in0=w1a_acc, in1=PA[:, 0:1],
                                     op=ALU.add), waits=[tok_w1m[c]])
        tok_w1a[c] = ve(lambda e: e.tensor_tensor(
            out=w1b_acc, in0=w1b_acc, in1=PB[0:32, 0:1], op=ALU.add),
            waits=[tok_w1m[c]])
    ve(lambda e: e.tensor_scalar(out=negw1a[:], in0=w1a_acc, scalar1=-1.0,
                                 scalar2=None, op0=ALU.mult),
       waits=[tok_w1a[CH - 1]])
    ve(lambda e: e.tensor_scalar(out=negw1b[:], in0=w1b_acc, scalar1=-1.0,
                                 scalar2=None, op0=ALU.mult),
       waits=[tok_w1a[CH - 1]])
    S.barrier()

    for lt in range(NL):
        lsl = slice(LT * lt, LT * (lt + 1))
        tok_xtr, tok_xtf, tok_sq, tok_mm, tok_xpw, tok_ad = {}, {}, {}, {}, {}, {}
        ve(lambda e: e.memset(xpdt_acc, 0.0))
        ve(lambda e: e.memset(xpbc_acc, 0.0))
        ve(lambda e: e.memset(rowm[:], 0.0))
        ve(lambda e: e.memset(rowt[:], 0.0))
        for c in range(CH):
            g = tok_mm.get(c - 2)
            tok_xtr[c] = dma("sync", xtr_ring[:, c % 2, :],
                             xTr_h[128 * c:128 * (c + 1), lsl],
                             waits=[g] if g else [], sem="LXR%d" % (c % 2))
            tok_xtf[c] = dma("sync", xtf_ring[:, c % 2, :],
                             xTf_h[128 * c:128 * (c + 1), lsl],
                             waits=[g] if g else [], sem="LX%d" % (c % 2))
            tok_sq[c] = ve(lambda e, c=c: e.tensor_tensor(
                out=dA_ring[:, c % 2, :], in0=xtf_ring[:, c % 2, :],
                in1=xtf_ring[:, c % 2, :], op=ALU.mult),
                waits=[tok_xtf[c]] + ([g] if g else []))
            tok_xpw[c] = dma("gpsimd", xpw_ring[:, c % 2, :],
                             xpwT_h[128 * c:128 * (c + 1), :],
                             waits=[g] if g else [], sem="LP%d" % (c % 2))
            tok_xpwf = dma("gpsimd", xpwF_ring[:, c % 2, :],
                           xpwTF_h[128 * c:128 * (c + 1), :],
                           waits=[g] if g else [], sem="LPF%d" % (c % 2))
            gad = tok_ad.get(c - 1)
            tok_mm[c] = te(lambda e, c=c: (
                e.matmul(PA[:], xpw_ring[:, c % 2, 0:128], xtr_ring[:, c % 2, :],
                         start=True, stop=True),
                e.matmul(PB[0:32, :], xpwF_ring[:, c % 2, 128:160],
                         xtf_ring[:, c % 2, :],
                         start=True, stop=True),
                e.matmul(PC[0:1, :], ocmF_s[:], xtf_ring[:, c % 2, :],
                         start=True, stop=True),
                e.matmul(PD[0:1, :], ocmF_s[:], dA_ring[:, c % 2, :],
                         start=True, stop=True))[-1],
                waits=[tok_xtr[c], tok_sq[c], tok_xpw[c], tok_xpwf] +
                      ([gad] if gad else []))
            ve(lambda e: e.tensor_tensor(out=xpdt_acc, in0=xpdt_acc, in1=PA[:],
                                         op=ALU.add), waits=[tok_mm[c]])
            ve(lambda e: e.tensor_tensor(out=xpbc_acc, in0=xpbc_acc,
                                         in1=PB[0:32, :], op=ALU.add),
               waits=[tok_mm[c]])
            ve(lambda e: e.tensor_tensor(out=rowm[:], in0=rowm[:], in1=PC[0:1, :],
                                         op=ALU.add), waits=[tok_mm[c]])
            tok_ad[c] = ve(lambda e: e.tensor_tensor(
                out=rowt[:], in0=rowt[:], in1=PD[0:1, :], op=ALU.add),
                waits=[tok_mm[c]])
        last = tok_ad[CH - 1]
        S.barrier()
        sc(lambda e, lt=lt: e.copy(out=xpraw_dt[:, lt, :], in_=xpdt_acc),
           waits=[last])
        sc(lambda e, lt=lt: e.copy(out=xpraw_bc[:, lt, :], in_=xpbc_acc),
           waits=[last])
        t_m = last
        t_q = last
        t_v1 = ve(lambda e: e.tensor_tensor(
            out=rowt2[:], in0=rowm[:], in1=rowm[:], op=ALU.mult), waits=[t_m])
        t_v2 = ve(lambda e: e.tensor_tensor(out=rowt[:], in0=rowt[:], in1=rowt2[:],
                                            op=ALU.subtract), waits=[t_q, t_v1])
        t_v2b = ve(lambda e: e.tensor_scalar(out=rowt[:], in0=rowt[:], scalar1=EPS,
                                             scalar2=None, op0=ALU.add),
                   waits=[t_v2])
        t_v3 = sc(lambda e: e.activation(out=rowt[:], in_=rowt[:], func=AF.Sqrt),
                  waits=[t_v2b])
        t_v4 = ve(lambda e: e.reciprocal(out=rowt2[:], in_=rowt[:]), waits=[t_v3])
        t_v5 = ve(lambda e: e.tensor_tensor(out=rowt[:], in0=rowm[:],
                                            in1=rowt2[:], op=ALU.mult),
                  waits=[t_v4])
        dma("gpsimd", rows_h[0:1, lsl], rowm[:], waits=[t_v1], sem="LG")
        dma("gpsimd", rows_h[1:2, lsl], rowt2[:], waits=[t_v4], sem="LG")
        dma("gpsimd", rows_h[2:3, lsl], rowt[:], waits=[t_v5], sem="LG")
        S.barrier()

    tok_spill = {}
    for lt in range(NL):
        lsl = slice(LT * lt, LT * (lt + 1))
        tb1 = dma("sync", mbc1[:], rows_h[0:1, lsl].broadcast_to((128, LT)))
        tb2 = dma("sync", rstdbc1[:], rows_h[1:2, lsl].broadcast_to((128, LT)))
        tb3 = dma("sync", mrbc1[:], rows_h[2:3, lsl].broadcast_to((128, LT)))
        tc1 = ve(lambda e, lt=lt: e.tensor_tensor(
            out=dtin_t1, in0=xpraw_dt[:, lt, :], in1=rstdbc1[:], op=ALU.mult),
            waits=[tb2])
        tc2 = ve(lambda e: e.scalar_tensor_tensor(
            out=dt_inT, in0=mrbc1[:], scalar=negw1a[:], in1=dtin_t1,
            op0=ALU.mult, op1=ALU.add), waits=[tb3, tc1])
        tc3 = ve(lambda e, lt=lt: e.tensor_tensor(
            out=bct_t1, in0=xpraw_bc[:, lt, :], in1=rstdbc1[0:32, :],
            op=ALU.mult), waits=[tb2])
        tc4 = ve(lambda e: e.scalar_tensor_tensor(
            out=BCT[:], in0=mrbc1[0:32, :], scalar=negw1b[:], in1=bct_t1,
            op0=ALU.mult, op1=ALU.add), waits=[tc3])
        t_bctd = dma("gpsimd", bct_h2[:, lsl], BCT[:], waits=[tc4], sem="LG")
        t_bbc, t_cbc = {}, {}
        for n in range(N):
            t_bbc[n] = dma("sync", B_bc[:, n, :],
                           bct_h2[n:n + 1, lsl].broadcast_to((128, LT)),
                           waits=[t_bctd])
            t_cbc[n] = dma("sync", C_bc[:, n, :],
                           bct_h2[16 + n:17 + n, lsl].broadcast_to((128, LT)),
                           waits=[t_bctd])
        S.barrier()
        ve(lambda e: e.memset(rowm[:], 0.0))
        ve(lambda e: e.memset(rowt[:], 0.0))

        tok_dtT, tok_tsp, tok_u, tok_xnT = {}, {}, {}, {}
        tok_dwl, tok_tmm = {}, {}
        tok_dA, tok_scan, tok_hst, tok_ym, tok_ymm = {}, {}, {}, {}, {}
        tok_x1c, tok_sq2, tok_st2, tok_xres = {}, {}, {}, {}
        for c in range(CH):
            g1 = tok_xnT.get(c - 2)
            g2 = tok_xres.get(c - 2)
            tok_xtf = dma("sync", xtf_ring[:, c % 2, :],
                          xTf_h[128 * c:128 * (c + 1), lsl],
                          waits=[t for t in (g1, g2) if t],
                          sem="LX%d" % (c % 2))
            gdw = tok_tmm.get(c - 2) if True else None
            tok_dwl[c] = dma("gpsimd", dtw_ring[:, c % 2, :],
                             dtwT_h[:, 128 * c:128 * (c + 1)],
                             waits=[gdw] if gdw else [], sem="LDW%d" % (c % 2))
            gPC = tok_tsp.get(c - 1)
            gu = tok_u.get(c - 2)
            tok_tmm[c] = tmm = te(lambda e, c=c: e.matmul(
                PC[:], dtw_ring[:, c % 2, :], dt_inT, start=True, stop=True),
                waits=[tc2, tok_dwl[c]] + [t for t in (gPC, gu) if t])
            gda = tok_dA.get((c - 2, N - 1))
            tok_tsp[c] = ve(lambda e, c=c: e.tensor_scalar(
                out=dtT_ring[:, c % 2, :], in0=PC[:], scalar1=dtb_s[:, c:c + 1],
                scalar2=None, op0=ALU.add),
                waits=[tmm] + ([gda] if gda else []))
            tsp2 = sc(lambda e, c=c: e.activation(
                out=dtT_ring[:, c % 2, :], in_=dtT_ring[:, c % 2, :],
                func=AF.Exp), waits=[tok_tsp[c]])
            tok_dtT[c] = sc(lambda e, c=c: e.activation(
                out=dtT_ring[:, c % 2, :], in_=dtT_ring[:, c % 2, :],
                func=AF.Ln, bias=1.0), waits=[tsp2])
            t_xn1 = ve(lambda e, c=c: e.tensor_tensor(
                out=xn_t3[:], in0=xtf_ring[:, c % 2, :], in1=mbc1[:],
                op=ALU.subtract), waits=[tok_xtf, tb1])
            tok_xnT[c] = ve(lambda e, c=c: e.tensor_tensor(
                out=xnT_ring[:, c % 2, :], in0=xn_t3[:], in1=rstdbc1[:],
                op=ALU.mult), waits=[t_xn1, tb2])
            tok_u[c] = ve(lambda e, c=c: e.tensor_tensor(
                out=u_ring[:, c % 2, :], in0=dtT_ring[:, c % 2, :],
                in1=xnT_ring[:, c % 2, :], op=ALU.mult),
                waits=[tok_dtT[c], tok_xnT[c]])
            Py = PA if c % 2 == 0 else PB
            for n in range(N):
                col = c * N + n

                def slot_prev(d, c=c, n=n):
                    return d.get((c, n - 2)) or d.get((c - 1, N - 2 + n))

                ga = slot_prev(tok_scan)
                tok_dA[(c, n)] = sc(lambda e, c=c, n=n, col=col: e.activation(
                    out=dA_ring[:, n % 2, :], in_=dtT_ring[:, c % 2, :],
                    func=AF.Exp, scale=negA[:, col:col + 1]),
                    waits=[tok_dtT[c]] + ([ga] if ga else []))
                gb = slot_prev(tok_scan)
                t_dbx = ve(lambda e, c=c, n=n: e.tensor_tensor(
                    out=dBx_ring[:, n % 2, :], in0=u_ring[:, c % 2, :],
                    in1=B_bc[:, n, :], op=ALU.mult),
                    waits=[tok_u[c], t_bbc[n]] + ([gb] if gb else []))
                gh = slot_prev(tok_hst)
                tok_scan[(c, n)] = ve(lambda e, n=n, col=col: e.tensor_tensor_scan(
                    out=h_ring[:, n % 2, :], data0=dA_ring[:, n % 2, :],
                    data1=dBx_ring[:, n % 2, :], initial=hst[:, col:col + 1],
                    op0=ALU.mult, op1=ALU.add),
                    waits=[tok_dA[(c, n)], t_dbx] + ([gh] if gh else []))
                tok_hst[(c, n)] = sc(lambda e, n=n, col=col: e.copy(
                    out=hst[:, col:col + 1],
                    in_=h_ring[:, n % 2, LT - 1:LT]), waits=[tok_scan[(c, n)]])
                gy = slot_prev(tok_ymm)
                tok_ym[(c, n)] = ve(lambda e, n=n: e.tensor_tensor(
                    out=ytmp_ring[:, n % 2, :], in0=h_ring[:, n % 2, :],
                    in1=C_bc[:, n, :], op=ALU.mult),
                    waits=[tok_scan[(c, n)], t_cbc[n]] + ([gy] if gy else []))
                gp = tok_x1c.get(c - 2) if n == 0 else None
                tok_ymm[(c, n)] = te(lambda e, n=n, Py=Py: e.matmul(
                    Py[:], idH_s[:], ytmp_ring[:, n % 2, :],
                    start=(n == 0), stop=False),
                    waits=[tok_ym[(c, n)]] + ([gp] if gp else []))
            tok_xres[c] = te(lambda e, c=c, Py=Py: e.matmul(
                Py[:], idF_s[:], xtf_ring[:, c % 2, :], start=False, stop=True),
                waits=[tok_ymm[(c, N - 1)], tok_xtf])
            gsp = tok_spill.get((lt, c - 2))
            gst = tok_st2.get(c - 2)
            tok_x1c[c] = ve(lambda e, c=c, Py=Py: e.scalar_tensor_tensor(
                out=x1c_ring[:, c % 2, :], in0=xnT_ring[:, c % 2, :],
                scalar=Dt_s[:, c:c + 1], in1=Py[:], op0=ALU.mult, op1=ALU.add),
                waits=[tok_xres[c]] + [t for t in (gsp, gst) if t])
            tok_spill[(lt, c)] = dma("gpsimd", x1sp_h[128 * c:128 * (c + 1), lsl],
                                     x1c_ring[:, c % 2, :],
                                     waits=[tok_x1c[c]], sem="LG")
            gs2 = tok_st2.get(c - 2)
            tok_sq2[c] = ve(lambda e, c=c: e.tensor_tensor(
                out=sq2_ring[:, c % 2, :], in0=x1c_ring[:, c % 2, :],
                in1=x1c_ring[:, c % 2, :], op=ALU.mult),
                waits=[tok_x1c[c]] + ([gs2] if gs2 else []))
            gs2b = tok_st2.get(c - 1)
            t_s2m = te(lambda e, c=c: (
                e.matmul(PE_[0:1, :], ocmH_s[:], x1c_ring[:, c % 2, :],
                         start=True, stop=True),
                e.matmul(PF[0:1, :], ocmH_s[:], sq2_ring[:, c % 2, :],
                         start=True, stop=True))[-1],
                waits=[tok_x1c[c], tok_sq2[c]] + ([gs2b] if gs2b else []))
            ve(lambda e: e.tensor_tensor(out=rowm[:], in0=rowm[:], in1=PE_[0:1, :],
                                         op=ALU.add), waits=[t_s2m])
            tok_st2[c] = ve(lambda e: e.tensor_tensor(
                out=rowt[:], in0=rowt[:], in1=PF[0:1, :], op=ALU.add),
                waits=[t_s2m])
        t_m2 = tok_st2[CH - 1]
        t_q2 = t_m2
        t_u1 = ve(lambda e: e.tensor_tensor(
            out=rowt2[:], in0=rowm[:], in1=rowm[:], op=ALU.mult), waits=[t_m2])
        t_u2 = ve(lambda e: e.tensor_tensor(out=rowt[:], in0=rowt[:], in1=rowt2[:],
                                            op=ALU.subtract), waits=[t_q2, t_u1])
        t_u2b = ve(lambda e: e.tensor_scalar(out=rowt[:], in0=rowt[:], scalar1=EPS,
                                             scalar2=None, op0=ALU.add),
                   waits=[t_u2])
        t_u3 = sc(lambda e: e.activation(out=rowt[:], in_=rowt[:], func=AF.Sqrt),
                  waits=[t_u2b])
        t_u4 = ve(lambda e: e.reciprocal(out=rowt2[:], in_=rowt[:]), waits=[t_u3])
        dma("gpsimd", rows_h[3:4, lsl], rowm[:], waits=[t_u1], sem="LG")
        dma("gpsimd", rows_h[4:5, lsl], rowt2[:], waits=[t_u4], sem="LG")
        S.barrier()

        tb4 = dma("sync", m2bc[:], rows_h[3:4, lsl].broadcast_to((128, LT)))
        tb5 = dma("sync", rstd2bc[:], rows_h[4:5, lsl].broadcast_to((128, LT)))
        S.barrier()
        tok_zn, tok_qs, tok_rr = {}, {}, {}
        t_Tmm = None
        for c in range(CH):
            gq = tok_qs.get(c - 2)
            tok_rr[c] = dma("sync", x1rr_ring[:, c % 2, :],
                            x1sp_h[128 * c:128 * (c + 1), lsl],
                            waits=[tok_spill[(lt, c)]] + ([gq] if gq else []),
                            sem="LR%d" % (c % 2))
            tz1 = ve(lambda e, c=c: e.tensor_tensor(
                out=xn_t3[:], in0=x1rr_ring[:, c % 2, :], in1=m2bc[:],
                op=ALU.subtract), waits=[tok_rr[c], tb4])
            tok_zn[c] = ve(lambda e, c=c: e.tensor_tensor(
                out=znT[:, c, :], in0=xn_t3[:], in1=rstd2bc[:], op=ALU.mult),
                waits=[tz1, tb5])
            tq0 = ve(lambda e, c=c, lsl=lsl: e.tensor_tensor(
                out=xn_t3[:], in0=znT[:, c, :], in1=maskbc_s[:, lsl],
                op=ALU.mult), waits=[tok_zn[c]])
            tq1 = ve(lambda e: e.tensor_reduce(
                out=qpart[:], in_=xn_t3[:], axis=mybir.AxisListType.X,
                op=ALU.add), waits=[tq0])
            tok_qs[c] = ve(lambda e, c=c: e.tensor_tensor(
                out=q_acc[:, c:c + 1], in0=q_acc[:, c:c + 1], in1=qpart[:],
                op=ALU.add), waits=[tq1])
            t_Tmm = te(lambda e, c=c: e.matmul(
                PC[:], AcatH[:, c, :], znT[:, c, :],
                start=(c == 0), stop=(c == CH - 1)), waits=[tok_zn[c]])
        sc(lambda e, lsl=lsl: e.copy(out=T_T[:, lsl], in_=PC[:]), waits=[t_Tmm])
        tok_wf, tok_wh, tok_stg, tok_out, tok_xid = {}, {}, {}, {}, {}
        for oc in range(OC):
            gwh = tok_stg.get(oc - 1)
            pw = tok_wh.get(oc - 1)
            wfa = dma("sync", Wf[:], wt4_h[oc, :, 0:CH // 2, :],
                      waits=[pw] if pw else [], sem="LW")
            wca = sc(lambda e: e.copy(out=Wh[:, 0:CH // 2], in_=Wf[:]),
                     waits=[wfa] + ([gwh] if gwh else []))
            wfb = dma("sync", Wf[:], wt4_h[oc, :, CH // 2:CH, :], waits=[wca],
                      sem="LW")
            tok_wh[oc] = sc(lambda e: e.copy(out=Wh[:, CH // 2:CH], in_=Wf[:]),
                            waits=[wfb])
            Po = PA if oc % 2 == 0 else PB
            gpo = tok_stg.get(oc - 2)
            t_base = None
            for c in range(CH):
                t_base = te(lambda e, c=c, oc=oc, Po=Po: e.matmul(
                    Po[:], Wh[:, c, :], znT[:, c, :],
                    start=(c == 0), stop=False),
                    waits=[tok_wh[oc], tok_zn[c]] +
                          ([gpo] if (c == 0 and gpo) else []))
            grr = tok_qs[CH - 1] if oc < 2 else tok_xid.get(oc - 2)
            t_rr2 = dma("gpsimd", x1rr_ring[:, oc % 2, :],
                        x1sp_h[128 * oc:128 * (oc + 1), lsl],
                        waits=[tok_spill[(lt, oc)]] + ([grr] if grr else []),
                        sem="LR2%d" % (oc % 2))
            tok_xid[oc] = te(lambda e, oc=oc, Po=Po: e.matmul(
                Po[:], idH_s[:], x1rr_ring[:, oc % 2, :],
                start=False, stop=True), waits=[t_base, t_rr2])
            gsg = tok_out.get(oc - 2)
            tok_stg[oc] = ve(lambda e, oc=oc, Po=Po: e.tensor_scalar(
                out=stg_ring[:, oc % 2, :], in0=Po[:], scalar1=b_s[:, oc:oc + 1],
                scalar2=None, op0=ALU.add),
                waits=[tok_xid[oc]] + ([gsg] if gsg else []))
            tok_out[oc] = dma("gpsimd", outT_h[128 * oc:128 * (oc + 1), lsl],
                              stg_ring[:, oc % 2, :],
                              waits=[tok_stg[oc]], sem="OP%d" % (oc % 2))
        S.barrier()

    t_lg = None
    for c in range(CH):
        t_lg = te(lambda e, c=c: e.matmul(PC[0:1, 0:8], q_acc[:, c:c + 1],
                                          rW_s[:, 8 * c:8 * (c + 1)],
                                          start=(c == 0), stop=(c == CH - 1)),
                  waits=[tok_qs[CH - 1]])
    t_l1 = ve(lambda e: e.tensor_tensor(out=logit_s[:], in0=PC[0:1, 0:8],
                                        in1=rb_s[:], op=ALU.add), waits=[t_lg])
    t_l2 = ve(lambda e: e.tensor_reduce(out=sm_m[:], in_=logit_s[:], axis=mybir.AxisListType.X,
                                        op=ALU.max), waits=[t_l1])
    t_l3 = ve(lambda e: e.tensor_scalar(out=sm_t[:], in0=logit_s[:],
                                        scalar1=sm_m[:], scalar2=None,
                                        op0=ALU.subtract), waits=[t_l2])
    t_l4 = sc(lambda e: e.activation(out=sm_t[:], in_=sm_t[:], func=AF.Exp),
              waits=[t_l3])
    t_l5 = ve(lambda e: e.tensor_reduce(out=sm_s[:], in_=sm_t[:], axis=mybir.AxisListType.X,
                                        op=ALU.add), waits=[t_l4])
    t_l6 = ve(lambda e: e.reciprocal(out=sm_s[:], in_=sm_s[:]), waits=[t_l5])
    t_l7 = ve(lambda e: e.tensor_scalar(out=routing[:], in0=sm_t[:],
                                        scalar1=sm_s[:], scalar2=None,
                                        op0=ALU.mult), waits=[t_l6])
    t_r1 = te(lambda e: e.matmul(PC[:, 0:8], on1F_s[:], routing[:],
                                 start=True, stop=True), waits=[t_l7])
    t_r2 = sc(lambda e: e.copy(out=routing_bc[:], in_=PC[:, 0:8]), waits=[t_r1])
    t_r3 = ve(lambda e: e.tensor_tensor(out=rexp_t[:], in0=selbc_s[:],
                                        in1=routing_bc[:], op=ALU.mult),
              waits=[t_r2])
    t_r4 = ve(lambda e: e.tensor_reduce(out=rexp[:], in_=rexp_t[:], axis=mybir.AxisListType.X,
                                        op=ALU.add), waits=[t_r3])
    t_T2 = ve(lambda e: e.tensor_scalar(out=T_T[:], in0=T_T[:], scalar1=rexp[:],
                                        scalar2=None, op0=ALU.mult),
              waits=[t_r4])
    t_bc1 = dma("sync", Wf[:], bcat_h[:, 0:CH // 2, :])
    t_bc1b = sc(lambda e: e.copy(out=BcatH[:, 0:CH // 2], in_=Wf[:]), waits=[t_bc1])
    t_bc1c = dma("sync", Wf[:], bcat_h[:, CH // 2:CH, :], waits=[t_bc1b])
    t_bc2 = sc(lambda e: e.copy(out=BcatH[:, CH // 2:CH], in_=Wf[:]), waits=[t_bc1c])
    tok_rbf, tok_wbr, tok_wb = {}, {}, {}
    for oc in range(OC):
        for ltb in range(NL):
            k = oc * NL + ltb
            Po = PA if k % 2 == 0 else PB
            lsl = slice(LT * ltb, LT * (ltb + 1))
            gv = tok_wbr.get(k - 2)
            t_lmm = te(lambda e, oc=oc, lsl=lsl, Po=Po: e.matmul(
                Po[:], BcatH[:, oc, :], T_T[:, lsl], start=True, stop=True),
                waits=[t_T2, t_bc2] + ([gv] if gv else []))
            tok_rbf[k] = dma("sync", rbf_ring[:, k % 2, :],
                             outT_h[128 * oc:128 * (oc + 1), lsl],
                             waits=[gv] if gv else [], sem="LRB%d" % (k % 2))
            gw = tok_wb.get(k - 2)
            tok_wbr[k] = ve(lambda e, k=k, Po=Po: e.tensor_tensor(
                out=wb_ring[:, k % 2, :], in0=Po[:], in1=rbf_ring[:, k % 2, :],
                op=ALU.add), waits=[t_lmm, tok_rbf[k]] + ([gw] if gw else []))
            tok_wb[k] = dma("gpsimd", outT_h[128 * oc:128 * (oc + 1), lsl],
                            wb_ring[:, k % 2, :], waits=[tok_wbr[k]],
                            sem="WB%d" % (k % 2))
    if DEBUG:
        dbgm_h = nc.dram_tensor("dbgm", [128, L + 64], F32, kind="ExternalOutput").ap()
        dma("gpsimd", dbgm_h[:, 0:L], T_T[:], waits=[t_T2], sem="ST")
        dma("sync", dbgm_h[:, L:L + 32], q_acc[:], waits=[tok_qs[CH - 1]], sem="ST")
        dma("sync", dbgm_h[0:1, L + 32:L + 40], routing[:], waits=[t_l7], sem="ST")
    S.add("sync", lambda e: None,
          waits=[(s, S.cnt.get(s, 0)) for s in
                 ("WB0", "WB1", "OP0", "OP1") if S.cnt.get(s, 0)])

    all_sems = set()
    for e in ENGINES:
        for fn, waits, inc, dmaidx in S.ops[e]:
            for (s, v) in waits:
                all_sems.add(s)
            if inc is not None:
                all_sems.add(inc[0])
    for s in sorted(all_sems):
        _sem(s)
    with nc.Block() as block:
        def emit(name):
            def run(eng):
                hw = {}
                for fn, waits, inc, dmaidx in S.ops[name]:
                    for (s, v) in waits:
                        if s == "B_" + name:
                            continue
                        if hw.get(s, -1) >= v:
                            continue
                        eng.wait_ge(sems[s], v)
                        hw[s] = v
                    if probe is not None and dmaidx is not None:
                        n0 = sum(1 for _ in nc.all_instructions())
                        r = fn(eng)
                        n1 = sum(1 for _ in nc.all_instructions())
                        probe[dmaidx] = 16 * (n1 - n0)
                    else:
                        r = fn(eng)
                    if inc is not None:
                        s, amt = inc
                        if r is None:
                            eng.sem_inc(sems[s], amt)
                        else:
                            r.then_inc(sems[s], amt)
            return run

        block.sync(emit("sync"))
        block.gpsimd(emit("gpsimd"))
        block.vector(emit("vector"))
        block.scalar(emit("scalar"))
        block.tensor(emit("tensor"))

    ctx.close()
    return nc


_NC_CACHE = None
_LAST_RES = None


def kernel(L0_attn_x, L0_attn_A_log, L0_attn_D, L0_attn_x_proj_w,
           L0_attn_dt_proj_w, L0_attn_dt_proj_b, L0_ffn_W, L0_ffn_b,
           L0_ffn_A, L0_ffn_B, L0_ffn_router_W, L0_ffn_router_b):
    global _NC_CACHE
    x = np.asarray(L0_attn_x, dtype=np.float32)
    A_log = np.asarray(L0_attn_A_log, dtype=np.float32)
    D = np.asarray(L0_attn_D, dtype=np.float32)
    xpw = np.asarray(L0_attn_x_proj_w, dtype=np.float32)
    dtw = np.asarray(L0_attn_dt_proj_w, dtype=np.float32)
    dtb = np.asarray(L0_attn_dt_proj_b, dtype=np.float32)
    W = np.asarray(L0_ffn_W, dtype=np.float32)
    bf = np.asarray(L0_ffn_b, dtype=np.float32)
    Al = np.asarray(L0_ffn_A, dtype=np.float32)
    Bl = np.asarray(L0_ffn_B, dtype=np.float32)
    rW = np.asarray(L0_ffn_router_W, dtype=np.float32)
    rb = np.asarray(L0_ffn_router_b, dtype=np.float32)

    xpwT = np.ascontiguousarray(xpw.T)
    dtwT = np.ascontiguousarray(dtw.T)
    alogT = np.ascontiguousarray(
        A_log.reshape(CH, 128, N).transpose(1, 0, 2).reshape(128, CH * N))
    dD = np.ascontiguousarray(D.reshape(CH, 128).T)
    wt4 = np.ascontiguousarray(
        W.T.reshape(CH, 128, OC, 128).transpose(2, 1, 0, 3))
    acat = Al.reshape(128, DM)
    acat3 = np.ascontiguousarray(
        acat.T.reshape(CH, 128, 128).transpose(1, 0, 2))
    bcat = np.ascontiguousarray(
        Bl.transpose(0, 2, 1).reshape(128, CH, 128))
    rw3 = np.ascontiguousarray(
        rW.T.reshape(CH, 128, 8).transpose(1, 0, 2).reshape(128, CH * 8))
    maskq = np.zeros((L,), np.float32)
    maskq[611:2047] = 1.0 / NS
    maskbc = np.ascontiguousarray(np.tile(maskq[None, :], (128, 1))).astype(np.float16)
    idF = np.eye(128, dtype=np.float32)
    idH = np.eye(128, dtype=np.float16)
    on1F = np.ones((1, 128), np.float32)
    on1H = np.ones((1, 128), np.float16)
    onesrow = np.ones((1, LT), np.float32)
    oc1R = np.ones((128, 1), np.float32)
    ocmR = np.full((128, 1), 1.0 / DM, np.float32)
    ocmH = np.full((128, 1), 1.0 / DM, np.float16)
    selbc = np.zeros((128, 8), np.float32)
    for p in range(128):
        selbc[p, p // 16] = 2.0

    if _NC_CACHE is None:
        amts = {}
        build_nc(probe=amts)
        amts = {k: v for k, v in amts.items() if v != 16}
        _NC_CACHE = build_nc(dma_amts=amts)
    nc = _NC_CACHE

    shared = dict(xpwT=xpwT, xpwTF=xpwT, dtwT=dtwT,
                  dtb=np.ascontiguousarray(dtb.reshape(CH, 128).T),
                  alogT=alogT, dD=dD, wt4=wt4,
                  brow=np.ascontiguousarray(bf.reshape(CH, 128).T), acat3=acat3,
                  bcat=bcat, rw3=rw3, rbrow=np.ascontiguousarray(rb[None, :]),
                  maskbc=maskbc, idF=idF, idH=idH, on1F=on1F, on1H=on1H,
                  onesrow=onesrow, oc1R=oc1R, ocmR=ocmR, ocmH=ocmH, ocmF=ocmR.copy(),
                  selbc=selbc)
    in_maps = []
    for core in range(8):
        beta = core // 2
        xT = np.ascontiguousarray(x[beta].T)
        m = dict(shared)
        m["xTr"] = xT
        m["xTf"] = xT
        in_maps.append(m)

    res = run_bass_kernel_spmd(nc, in_maps, core_ids=list(range(8)))
    global _LAST_RES
    _LAST_RES = res
    out = np.empty((B, L, DM), np.float32)
    for beta in range(B):
        out[beta] = res.results[2 * beta]["outT"].T
    return out


if __name__ == "__main__":
    import reference as refmod
    inp = refmod.setup_inputs(0)
    o = kernel(**{k: np.asarray(v) for k, v in inp.items()})
    r = np.asarray(refmod.reference(**inp))
    num = float(((o - r) ** 2).mean())
    den = float((r ** 2).mean())
    print("resvar:", num / den)
